# version 01
# baseline (speedup 1.0000x reference)
import math
import numpy as np
import concourse.bass as bass
import concourse.mybir as mybir
from concourse.bass_utils import run_bass_kernel_spmd

F32 = mybir.dt.float32
BF16 = mybir.dt.bfloat16
AF = mybir.ActivationFunctionType
ALU = mybir.AluOpType
AX = mybir.AxisListType.X

D = 4096
KC = 32
T = 1024
NH = 32
NJ = 8
NR = 4
E = 8
RL = 16
EPS = 1e-5
ISQ = 1.0 / math.sqrt(128.0)
NEG = -1.0e30
KVROWS = 2 * NH * 128 + 128


def att_tiles(j):
    tiles = [(rp, j) for rp in range(NR)]
    for rp in range(NR):
        tiles += [(rp, i) for i in range(j)]
    return tiles


def att_pieces(j):
    t = att_tiles(j)
    return [t[k:k + 8] for k in range(0, len(t), 8)]


def runs_of(piece):
    runs = []
    for (rp, i) in piece:
        if runs and runs[-1][0] == rp and runs[-1][2] == i:
            runs[-1][2] += 1
        else:
            runs.append([rp, i, i + 1])
    return runs


class Ctr:
    def __init__(self, sem, name):
        self.sem = sem
        self.name = name
        self.n = 0
    def inc(self, inst, k=1):
        inst.then_inc(self.sem, k)
        self.n += k
        return self.n


def build_program(dump=False, upto=99):
    nc = bass.Bass("TRN2", target_bir_lowering=False, debug=False)

    x_in = nc.dram_tensor("x_s", [128, KC, T], F32, kind="ExternalInput").ap()
    w_in = {}
    for nm in ("wq", "wk", "wv", "wo", "wb"):
        w_in[nm] = nc.dram_tensor(nm + "_s", [128, KC, KC, 128], F32, kind="ExternalInput").ap()
    rcat_in = nc.dram_tensor("rcat_s", [128, KC, 136], F32, kind="ExternalInput").ap()
    lb_in = nc.dram_tensor("lb_s", [128, KC, 128], F32, kind="ExternalInput").ap()
    bias_in = nc.dram_tensor("bias_s", [128, 6, KC], F32, kind="ExternalInput").ap()
    brl_in = nc.dram_tensor("brl_s", [128, E], F32, kind="ExternalInput").ap()
    id_in = nc.dram_tensor("id_s", [128, 128], F32, kind="ExternalInput").ap()
    ones_in = nc.dram_tensor("ones_s", [128, 128], F32, kind="ExternalInput").ap()
    bmask_in = nc.dram_tensor("bmask_s", [128, 512], F32, kind="ExternalInput").ap()
    out_ext = nc.dram_tensor("out_s", [128, KC, T], F32, kind="ExternalOutput").ap()

    x1_d = nc.dram_tensor("x1_d", [128, KC, T], F32).ap()
    x2_d = nc.dram_tensor("x2_d", [128, KC, T], F32).ap()
    q_d = nc.dram_tensor("q_d", [NH, 128, T], BF16).ap()
    kv_own = nc.dram_tensor("kv_own", [KVROWS, T], BF16).ap()
    kv_all = nc.dram_tensor("kv_all", [NR * KVROWS, T], BF16).ap()
    dumps = {}
    if dump:
        dumps["u1"] = nc.dram_tensor("dump_u1", [128, KC * T], BF16, kind="ExternalOutput").ap()
        dumps["x1"] = nc.dram_tensor("dump_x1", [128, KC, T], F32, kind="ExternalOutput").ap()
        dumps["y"] = nc.dram_tensor("dump_y", [128, KC * T], BF16, kind="ExternalOutput").ap()
        dumps["qkv"] = nc.dram_tensor("dump_qkv", [KVROWS + NH * 128, T], BF16, kind="ExternalOutput").ap()
        dumps["o"] = nc.dram_tensor("dump_o", [128, KC * T], BF16, kind="ExternalOutput").ap()
        dumps["x2"] = nc.dram_tensor("dump_x2", [128, KC, T], F32, kind="ExternalOutput").ap()

    import contextlib

    class DBuf:
        def __init__(self, t, n):
            self.t = t
            self.n = n
        def __getitem__(self, i):
            return self.t[:, i * self.n:(i + 1) * self.n]

    with contextlib.ExitStack() as stack:
        def sb(name, shape, dt):
            if len(shape) == 3:
                assert shape[0] == 2
                t = stack.enter_context(nc.sbuf_tensor(name, [shape[1], 2 * shape[2]], dt))
                return DBuf(t, shape[2])
            return stack.enter_context(nc.sbuf_tensor(name, shape, dt))
        def psb(name, shape, dt):
            return stack.enter_context(nc.psum_tensor(name, shape, dt))
        id_sb = sb("id_sb", [128, 128], BF16)
        ones_sb = sb("ones_sb", [128, 128], BF16)
        bmask_sb = sb("bmask_sb", [128, 512], BF16)
        rcat_b = sb("rcat_b", [128, KC * 136], BF16)
        lb_b = sb("lb_b", [128, KC * 128], BF16)
        bias_sb = sb("bias_sb", [128, 6 * KC], F32)
        biasx_sb = sb("biasx_sb", [128, 6 * KC], F32)
        brl_b = sb("brl_b", [128, E], F32)
        abuf = sb("abuf", [128, KC * T], BF16)
        mb_sb = sb("mb_sb", [128, T], F32)
        rstd_sb = sb("rstd_sb", [128, T], F32)
        tst_sb = sb("tst_sb", [128, T], F32)
        sqst = sb("sqst", [2, 128, T], BF16)
        xst = sb("xst", [2, 128, T], F32)
        estage = sb("estage", [2, 128, 512], F32)
        x1stage = sb("x1stage", [2, 128, T], F32)
        wslab = sb("wslab", [2, 128, KC * 128], BF16)
        pstage = sb("pstage", [2, 128, T], BF16)
        lgt = sb("lgt", [128, E], F32)
        rot = sb("rot", [128, E], F32)
        gex = sb("gex", [128, E], F32)
        tA = sb("tA", [128, E], F32)
        accn = sb("accn", [128, E], F32)
        accs = sb("accs", [128, E], F32)
        kz = sb("kz", [128, 1], F32)
        ssum = sb("ssum", [128, 1], F32)
        tau = sb("tau", [128, 1], F32)
        wsp = sb("wsp", [128, E], F32)
        rots = sb("rots", [128, 7 * E], F32)
        ges = sb("ges", [128, 7 * E], F32)
        gem = sb("gem", [128, 7 * E], F32)
        nred = sb("nred", [128, E], F32)
        sred = sb("sred", [128, E], F32)
        q1s = sb("q1s", [128, E], F32)
        q2s = sb("q2s", [128, E], F32)
        inds = sb("inds", [128, E], F32)
        szz = sb("szz", [128, E], F32)
        s2s = sb("s2s", [128, 1], F32)
        t1s = sb("t1s", [128, 1], F32)
        rks = sb("rks", [128, 1], F32)
        bm1 = sb("bm1", [128, NH], F32)
        bm2 = sb("bm2", [128, NH], F32)
        bm3 = sb("bm3", [128, NH], F32)
        negb2 = sb("negb2", [128, NH], F32)
        hw_sb = sb("hw_sb", [128, 128], BF16)
        hwT_sb = sb("hwT_sb", [128, T], BF16)
        qn2 = sb("qn2", [128, NH], F32)
        kn2 = sb("kn2", [128, NH], F32)
        kn2g = sb("kn2g", [128, NR * NH], F32)
        rmx = sb("rmx", [128, 1], F32)
        negb = sb("negb", [128, NH], F32)
        ksb = sb("ksb", [2, 128, NR * T], BF16)
        vsb = sb("vsb", [128, NR * T], BF16)
        qsb = sb("qsb", [2, 128, T], BF16)
        vt_sb = sb("vt_sb", [2, 128, NR * T], BF16)
        pbuf = sb("pbuf", [2, 128, T], BF16)
        ptbuf = sb("ptbuf", [2, 128, T], BF16)
        orow = sb("orow", [2, 128, T], BF16)
        dacc = sb("dacc", [128, 8], F32)
        dtot = sb("dtot", [128, 1], F32)
        drec = sb("drec", [128, 1], F32)
        ps_S = psb("ps_S", [128, 2048], F32)
        ps_O = psb("ps_O", [128, 512], F32)
        ps_T = psb("ps_T", [128, 2048], BF16)
        ps_M = psb("ps_M", [128, 512], F32)
        _sLW = stack.enter_context(nc.semaphore("sLW"))
        _sLX = stack.enter_context(nc.semaphore("sLX"))
        _sLK = stack.enter_context(nc.semaphore("sLK"))
        _sST = stack.enter_context(nc.semaphore("sST"))
        _sCC = stack.enter_context(nc.semaphore("sCC"))
        _sPM = stack.enter_context(nc.semaphore("sPM"))
        _sPT = stack.enter_context(nc.semaphore("sPT"))
        _sAE = stack.enter_context(nc.semaphore("sAE"))
        _sVE = stack.enter_context(nc.semaphore("sVE"))
        _sPH = stack.enter_context(nc.semaphore("sPH"))
        _sDV = stack.enter_context(nc.semaphore("sDV"))
        _sDS = stack.enter_context(nc.semaphore("sDS"))
        block = stack.enter_context(nc.Block())
        LW = Ctr(_sLW, "LW"); LX = Ctr(_sLX, "LX"); LK = Ctr(_sLK, "LK")
        ST = Ctr(_sST, "ST"); CC = Ctr(_sCC, "CC"); PM = Ctr(_sPM, "PM")
        PT = Ctr(_sPT, "PT"); AE = Ctr(_sAE, "AE"); VE = Ctr(_sVE, "VE")
        PH = Ctr(_sPH, "PH")
        DV = Ctr(_sDV, "DV")
        DS = Ctr(_sDS, "DS")

        def selfsync(eng, inst, ctr=None):
            c = ctr if ctr is not None else DV
            c.inc(inst)
            eng.wait_ge(c.sem, c.n)
        waits = []

        def Wt(eng, ctr, val):
            if val > 0:
                eng.wait_ge(ctr.sem, val)
                waits.append((ctr, val))

        def section(name, fn):
            dec = getattr(block, name)
            @dec
            def _(eng):
                fn(eng)

        def barrier():
            target = PH.n + 5
            for name in ("sync", "gpsimd", "tensor", "scalar", "vector"):
                def fn(eng, target=target):
                    eng.sem_inc(PH.sem, 1)
                    eng.wait_ge(PH.sem, target)
                section(name, fn)
            PH.n = target
            waits.append((PH, target))

        def p0_gp(g):
            g.dma_start(out=id_sb[:], in_=id_in).then_inc(LW.sem, 16)
            g.dma_start(out=ones_sb[:], in_=ones_in).then_inc(LW.sem, 16)
            g.dma_start(out=bmask_sb[:], in_=bmask_in).then_inc(LW.sem, 16)
            g.dma_start(out=rcat_b[:], in_=rcat_in.rearrange("p k c -> p (k c)")).then_inc(LW.sem, 16)
            g.dma_start(out=lb_b[:], in_=lb_in.rearrange("p k c -> p (k c)")).then_inc(LW.sem, 16)
        section("gpsimd", p0_gp)
        LW.n += 5 * 16

        def p0_sync(s):
            s.dma_start(out=bias_sb[:], in_=bias_in.rearrange("p a k -> p (a k)")).then_inc(LX.sem, 16)
            s.dma_start(out=brl_b[:], in_=brl_in).then_inc(LX.sem, 16)
        section("sync", p0_sync)
        LX.n += 32

        def p0_v(v):
            Wt(v, LX, 32)
            v.tensor_scalar(biasx_sb[:, 0:KC], bias_sb[:, 0:KC], ISQ, None, ALU.mult)
            v.tensor_copy(biasx_sb[:, KC:4 * KC], bias_sb[:, KC:4 * KC])
            v.tensor_scalar(biasx_sb[:, 4 * KC:5 * KC], bias_sb[:, 4 * KC:5 * KC], 0.5, None, ALU.mult)
            VE.inc(v.memset(biasx_sb[:, 5 * KC:6 * KC], 0.0))
        section("vector", p0_v)

        def p0_t(t):
            Wt(t, LW, 5 * 16)
            Wt(t, VE, VE.n)
        section("tensor", p0_t)
        barrier()

        def ln_phase(src_ap, st_ready=0, dump_to=None):
            lx0, pm0, ae0, ve0 = LX.n, PM.n, AE.n, VE.n

            def f_sync(s):
                Wt(s, ST, st_ready)
                for kc in range(KC):
                    if kc >= 2:
                        Wt(s, AE, ae0 + kc - 1)
                    s.dma_start(out=xst[kc % 2], in_=src_ap[:, kc, :]).then_inc(LX.sem, 16)
            section("sync", f_sync)
            LX.n += KC * 16

            def f_sc(sc):
                for kc in range(KC):
                    Wt(sc, LX, lx0 + (kc + 1) * 16)
                    AE.inc(sc.copy(abuf[:, kc * T:(kc + 1) * T], xst[kc % 2]))
            section("scalar", f_sc)

            def f_v(v):
                for kc in range(KC):
                    Wt(v, AE, ae0 + kc + 1)
                    if kc >= 2:
                        Wt(v, PM, pm0 + 4 * (kc - 1))
                    zb = abuf[:, kc * T:(kc + 1) * T]
                    VE.inc(v.tensor_tensor(sqst[kc % 2], zb, zb, ALU.mult))
            section("vector", f_v)

            def f_t(t):
                for kc in range(KC):
                    Wt(t, VE, ve0 + kc + 1)
                    zb = abuf[:, kc * T:(kc + 1) * T]
                    mm = None
                    for half in range(2):
                        t.matmul(ps_S[:, half * 512:(half + 1) * 512],
                                 ones_sb[:], zb[:, half * 512:(half + 1) * 512],
                                 start=(kc == 0), stop=(kc == KC - 1))
                        mm = t.matmul(ps_S[:, 1024 + half * 512:1024 + (half + 1) * 512],
                                      ones_sb[:], sqst[kc % 2][:, half * 512:(half + 1) * 512],
                                      start=(kc == 0), stop=(kc == KC - 1))
                    PM.inc(mm, 4)
            section("tensor", f_t)
            pm_stats = pm0 + 4 * KC

            def f_v2(v):
                Wt(v, PM, pm_stats)
                v.tensor_scalar(mb_sb[:], ps_S[:, 0:1024], 1.0 / D, None, ALU.mult)
                v.tensor_scalar(tst_sb[:], ps_S[:, 1024:2048], 1.0 / D, None, ALU.mult)
                v.tensor_tensor(rstd_sb[:], mb_sb[:], mb_sb[:], ALU.mult)
                v.tensor_tensor(tst_sb[:], tst_sb[:], rstd_sb[:], ALU.subtract)
                v.tensor_scalar(tst_sb[:], tst_sb[:], EPS, None, ALU.add)
                VE.inc(v.reciprocal(rstd_sb[:], tst_sb[:]))
            section("vector", f_v2)

            def f_sc2(sc):
                Wt(sc, VE, VE.n)
                AE.inc(sc.activation(rstd_sb[:], rstd_sb[:], AF.Sqrt))
            section("scalar", f_sc2)

            def f_v3(v):
                Wt(v, AE, AE.n)
                mm = None
                for kc in range(KC):
                    zb = abuf[:, kc * T:(kc + 1) * T]
                    v.tensor_tensor(tst_sb[:], zb, mb_sb[:], ALU.subtract)
                    mm = v.tensor_tensor(zb, tst_sb[:], rstd_sb[:], ALU.mult)
                VE.inc(mm)
            section("vector", f_v3)

            if dump_to is not None:
                def f_dp(s):
                    Wt(s, VE, VE.n)
                    s.dma_start(out=dump_to, in_=abuf[:]).then_inc(ST.sem, 16)
                section("sync", f_dp)
                ST.n += 16
            barrier()

        def ffn_rl():
            pm0, ve0, pt0 = PM.n, VE.n, PT.n

            def f_t(t):
                for tt in range(NJ):
                    off = 1024 + (tt % 2) * 512
                    ps_rl = ps_S[:, off:off + 136]
                    if tt >= 2:
                        Wt(t, VE, ve0 + (tt - 1) * 10)
                    mm = None
                    for kc in range(KC):
                        mm = t.matmul(ps_rl, abuf[:, kc * T + tt * 128:kc * T + (tt + 1) * 128],
                                      rcat_b[:, kc * 136:(kc + 1) * 136],
                                      start=(kc == 0), stop=(kc == KC - 1))
                    PM.inc(mm)
                    Wt(t, VE, ve0 + tt * 10 + 9)
                    if tt >= 2:
                        Wt(t, VE, ve0 + (tt - 2) * 10 + 10)
                    PT.inc(t.transpose(ps_T[:, (tt % 2) * 1024:(tt % 2) * 1024 + 128],
                                       hw_sb[:], id_sb[:]))
            section("tensor", f_t)

            def f_v(v):
                for tt in range(NJ):
                    off = 1024 + (tt % 2) * 512
                    ps_rl = ps_S[:, off:off + 136]
                    Wt(v, PM, pm0 + tt + 1)
                    ges3 = ges[:].rearrange("p (e j) -> p e j", j=7)
                    gem3 = gem[:].rearrange("p (e j) -> p e j", j=7)
                    rots3 = rots[:].rearrange("p (j e) -> p j e", j=7)
                    selfsync(v, v.tensor_tensor(lgt[:], ps_rl[:, 0:E], brl_b[:], ALU.add))
                    mm = None
                    for jj in range(1, E):
                        v.tensor_copy(rots[:, (jj - 1) * E:(jj - 1) * E + E - jj], lgt[:, jj:E])
                        mm = v.tensor_copy(rots[:, (jj - 1) * E + E - jj:jj * E], lgt[:, 0:jj])
                    selfsync(v, mm)
                    for jj in range(1, E):
                        mm = v.tensor_tensor(ges3[:, :, jj - 1], rots3[:, jj - 1, :], lgt[:], ALU.is_ge)
                    selfsync(v, mm)
                    for jj in range(1, E):
                        mm = v.tensor_tensor(gem3[:, :, jj - 1], ges3[:, :, jj - 1],
                                             rots3[:, jj - 1, :], ALU.mult)
                    selfsync(v, mm)
                    v.tensor_reduce(nred[:], ges3, AX, ALU.add)
                    selfsync(v, v.tensor_reduce(sred[:], gem3, AX, ALU.add))
                    selfsync(v, v.tensor_tensor(q1s[:], nred[:], lgt[:], ALU.mult))
                    selfsync(v, v.scalar_tensor_tensor(q2s[:], q1s[:], 1.0, sred[:], ALU.add, ALU.subtract))
                    selfsync(v, v.tensor_single_scalar(inds[:], q2s[:], 0.0, ALU.is_gt))
                    v.tensor_reduce(kz[:], inds[:], AX, ALU.add)
                    selfsync(v, v.tensor_tensor(szz[:], inds[:], lgt[:], ALU.mult))
                    v.tensor_reduce(s2s[:], szz[:], AX, ALU.add)
                    selfsync(v, v.reciprocal(rks[:], kz[:]))
                    selfsync(v, v.tensor_scalar(t1s[:], s2s[:], -1.0, None, ALU.add))
                    selfsync(v, v.tensor_tensor(tau[:], t1s[:], rks[:], ALU.mult))
                    selfsync(v, v.tensor_scalar(wsp[:], lgt[:], tau[:], 0.0, ALU.subtract, ALU.max))
                    mm = None
                    for e in range(E):
                        mm = v.tensor_scalar(hw_sb[:, e * RL:(e + 1) * RL],
                                             ps_rl[:, 8 + e * RL:8 + (e + 1) * RL],
                                             wsp[:, e:e + 1], None, ALU.mult)
                    VE.inc(mm, 9)
                    Wt(v, PT, pt0 + tt + 1)
                    VE.inc(v.tensor_copy(hwT_sb[:, tt * 128:(tt + 1) * 128],
                                         ps_T[:, (tt % 2) * 1024:(tt % 2) * 1024 + 128]))
            section("vector", f_v)
            assert VE.n == ve0 + NJ * 10
            barrier()

        def proj_res_sweep(wname, bias_col0, scale, expert, res_ap, dst_ap,
                           st_ready=0, dump_to=None):
            lw0, lx0, pm0, ae0, ve0, st0 = LW.n, LX.n, PM.n, AE.n, VE.n, ST.n

            def f_gp(g):
                for mc in range(KC):
                    if mc >= 2:
                        Wt(g, PM, pm0 + 2 * (mc - 1))
                    g.dma_start(out=wslab[mc % 2],
                                in_=w_in[wname][:, mc].rearrange("p k m -> p (k m)")).then_inc(LW.sem, 16)
            section("gpsimd", f_gp)
            LW.n += KC * 16

            def f_sync(s):
                Wt(s, ST, st_ready)
                for mc in range(KC):
                    if mc >= 2:
                        Wt(s, VE, ve0 + 2 * (mc - 1))
                    s.dma_start(out=xst[mc % 2], in_=res_ap[:, mc, :]).then_inc(LX.sem, 16)
                    if mc >= 1:
                        Wt(s, VE, ve0 + 2 * mc)
                        s.dma_start(out=dst_ap[:, mc - 1, :], in_=x1stage[(mc - 1) % 2]).then_inc(ST.sem, 16)
                Wt(s, VE, ve0 + 2 * KC)
                s.dma_start(out=dst_ap[:, KC - 1, :], in_=x1stage[(KC - 1) % 2]).then_inc(ST.sem, 16)
            section("sync", f_sync)
            LX.n += KC * 16
            ST.n += KC * 16

            def f_t(t):
                for mc in range(KC):
                    Wt(t, LW, lw0 + (mc + 1) * 16)
                    for tg in range(2):
                        gi = mc * 2 + tg
                        if gi >= 2:
                            Wt(t, AE, ae0 + gi - 1)
                        ps = ps_S[:, (gi % 2) * 512:(gi % 2) * 512 + 512]
                        nk = KC if expert else KC - 1
                        for kc in range(nk):
                            t.matmul(ps, wslab[mc % 2][:, kc * 128:(kc + 1) * 128],
                                     abuf[:, kc * T + tg * 512:kc * T + (tg + 1) * 512],
                                     start=(kc == 0), stop=False)
                        if expert:
                            mm = t.matmul(ps, lb_b[:, mc * 128:(mc + 1) * 128],
                                          hwT_sb[:, tg * 512:(tg + 1) * 512],
                                          start=False, stop=True)
                        else:
                            mm = t.matmul(ps, wslab[mc % 2][:, (KC - 1) * 128:KC * 128],
                                          abuf[:, (KC - 1) * T + tg * 512:(KC - 1) * T + (tg + 1) * 512],
                                          start=False, stop=True)
                        PM.inc(mm)
            section("tensor", f_t)

            def f_sc(sc):
                for mc in range(KC):
                    for tg in range(2):
                        gi = mc * 2 + tg
                        Wt(sc, PM, pm0 + gi + 1)
                        if gi >= 2:
                            Wt(sc, VE, ve0 + gi - 1)
                        ps = ps_S[:, (gi % 2) * 512:(gi % 2) * 512 + 512]
                        AE.inc(sc.activation(estage[gi % 2], ps, AF.Identity,
                                             bias=biasx_sb[:, bias_col0 + mc:bias_col0 + mc + 1],
                                             scale=scale))
            section("scalar", f_sc)

            def f_v(v):
                for mc in range(KC):
                    Wt(v, LX, lx0 + (mc + 1) * 16)
                    if mc >= 2:
                        Wt(v, ST, st0 + (mc - 1) * 16)
                    for tg in range(2):
                        gi = mc * 2 + tg
                        Wt(v, AE, ae0 + gi + 1)
                        VE.inc(v.tensor_tensor(x1stage[mc % 2][:, tg * 512:(tg + 1) * 512],
                                               estage[gi % 2],
                                               xst[mc % 2][:, tg * 512:(tg + 1) * 512], ALU.add))
            section("vector", f_v)

            if dump_to is not None:
                def f_dp(s):
                    Wt(s, ST, ST.n)
                    s.dma_start(out=dump_to.rearrange("p k t -> p (k t)"),
                                in_=dst_ap.rearrange("p k t -> p (k t)")).then_inc(ST.sem, 16)
                section("sync", f_dp)
                ST.n += 16
            barrier()

        def qkv_phase(dump_to=None):
            projs = ("wq", "wk", "wv")
            lw0, pm0, pt0, ae0, ve0, st0 = LW.n, PM.n, PT.n, AE.n, VE.n, ST.n
            NSL = 3 * KC

            def f_gp(g):
                for sl in range(NSL):
                    pi, mc = divmod(sl, KC)
                    if sl >= 2:
                        Wt(g, PM, pm0 + 2 * (sl - 1))
                    g.dma_start(out=wslab[sl % 2],
                                in_=w_in[projs[pi]][:, mc].rearrange("p k m -> p (k m)")).then_inc(LW.sem, 16)
            section("gpsimd", f_gp)
            LW.n += NSL * 16

            def norm_mms(t, snorm):
                if snorm // KC >= 2:
                    return
                Wt(t, AE, ae0 + 2 * (snorm * 2 + 2))
                if snorm >= 1:
                    Wt(t, VE, ve0 + snorm)
                mm = None
                for tg in range(2):
                    mm = t.matmul(ps_M[:, 0:512], ones_sb[:],
                                  sqst[snorm % 2][:, tg * 512:(tg + 1) * 512],
                                  start=(tg == 0), stop=(tg == 1))
                PT.inc(mm)

            def f_t(t):
                for sl in range(NSL):
                    Wt(t, LW, lw0 + (sl + 1) * 16)
                    for tg in range(2):
                        gi = sl * 2 + tg
                        if gi >= 2:
                            Wt(t, AE, ae0 + 2 * (gi - 1))
                        ps = ps_S[:, (gi % 2) * 512:(gi % 2) * 512 + 512]
                        mm = None
                        for kc in range(KC):
                            mm = t.matmul(ps, wslab[sl % 2][:, kc * 128:(kc + 1) * 128],
                                          abuf[:, kc * T + tg * 512:kc * T + (tg + 1) * 512],
                                          start=(kc == 0), stop=(kc == KC - 1))
                        PM.inc(mm)
                    if sl >= 1:
                        norm_mms(t, sl - 1)
                norm_mms(t, NSL - 1)
            section("tensor", f_t)

            def f_sc(sc):
                for sl in range(NSL):
                    pi, mc = divmod(sl, KC)
                    scale = ISQ if pi == 0 else 1.0
                    for tg in range(2):
                        gi = sl * 2 + tg
                        Wt(sc, PM, pm0 + gi + 1)
                        if sl >= 2 and tg == 0:
                            Wt(sc, ST, st0 + (sl - 1) * 16)
                        ps = ps_S[:, (gi % 2) * 512:(gi % 2) * 512 + 512]
                        last = sc.activation(pstage[sl % 2][:, tg * 512:(tg + 1) * 512], ps, AF.Identity,
                                             bias=biasx_sb[:, pi * KC + mc:pi * KC + mc + 1],
                                             scale=scale)
                        if pi < 2:
                            selfsync(sc, last, DS)
                            if sl >= 2:
                                Wt(sc, PT, pt0 + sl - 1)
                            last = sc.activation(sqst[sl % 2][:, tg * 512:(tg + 1) * 512],
                                                 pstage[sl % 2][:, tg * 512:(tg + 1) * 512], AF.Square)
                        AE.inc(last, 2)
            section("scalar", f_sc)

            def f_v(v):
                for snorm in range(2 * KC):
                    pi, mc = divmod(snorm, KC)
                    Wt(v, PT, pt0 + snorm + 1)
                    selfsync(v, v.tensor_reduce(rmx[:], ps_M[:, 0:512], AX, ALU.max))
                    dstn = qn2 if pi == 0 else kn2
                    VE.inc(v.tensor_copy(dstn[:, mc:mc + 1], rmx[:]))
            section("vector", f_v)

            def f_sync(s):
                for sl in range(NSL):
                    pi, mc = divmod(sl, KC)
                    Wt(s, AE, ae0 + 2 * (sl * 2 + 2))
                    if pi == 0:
                        dst = q_d[mc]
                    elif pi == 1:
                        dst = kv_own[mc * 128:(mc + 1) * 128, :]
                    else:
                        dst = kv_own[NH * 128 + mc * 128:NH * 128 + (mc + 1) * 128, :]
                    s.dma_start(out=dst, in_=pstage[sl % 2]).then_inc(ST.sem, 16)
            section("sync", f_sync)
            ST.n += NSL * 16

            def f_gp2(g):
                Wt(g, VE, ve0 + 2 * KC)
                g.dma_start(out=kv_own[2 * NH * 128:2 * NH * 128 + 128, 0:NH],
                            in_=kn2[:]).then_inc(ST.sem, 16)
            section("gpsimd", f_gp2)
            ST.n += 16

            if dump_to is not None:
                def f_dp(s):
                    Wt(s, ST, ST.n)
                    s.dma_start(out=dump_to[0:KVROWS, :], in_=kv_own[:, :]).then_inc(ST.sem, 16)
                    s.dma_start(out=dump_to[KVROWS:KVROWS + NH * 128, :],
                                in_=q_d.rearrange("h p t -> (h p) t")).then_inc(ST.sem, 16)
                section("sync", f_dp)
                ST.n += 32
            barrier()

        def attention_phase(st_qkv, dump_to=None):
            def f_cc(g):
                Wt(g, ST, st_qkv)
                CC.inc(g.collective_compute(
                    "AllGather", ALU.bypass,
                    replica_groups=[[0, 1, 2, 3], [4, 5, 6, 7]],
                    ins=[kv_own.opt()],
                    outs=[kv_all.opt()],
                ))
                g.wait_ge(CC.sem, CC.n)
            section("gpsimd", f_cc)
            barrier()

            def f_g0(g):
                for rp in range(NR):
                    g.dma_start(out=kn2g[:, rp * NH:(rp + 1) * NH],
                                in_=kv_all[rp * KVROWS + 2 * NH * 128:rp * KVROWS + 2 * NH * 128 + 128, 0:NH]).then_inc(LK.sem, 16)
            section("gpsimd", f_g0)
            LK.n += NR * 16

            def f_v0(v):
                Wt(v, LK, LK.n)
                v.tensor_tensor(bm1[:], kn2g[:, 0:NH], kn2g[:, NH:2 * NH], ALU.max)
                selfsync(v, v.tensor_tensor(bm2[:], kn2g[:, 2 * NH:3 * NH], kn2g[:, 3 * NH:4 * NH], ALU.max))
                selfsync(v, v.tensor_tensor(bm3[:], bm1[:], bm2[:], ALU.max))
                VE.inc(v.tensor_tensor(negb[:], bm3[:], qn2[:], ALU.mult))
            section("vector", f_v0)

            def f_sc0(sc):
                Wt(sc, VE, VE.n)
                selfsync(sc, sc.activation(negb2[:], negb[:], AF.Sqrt), DS)
                AE.inc(sc.mul(negb[:], negb2[:], -1.0))
            section("scalar", f_sc0)
            barrier()

            plan = []
            for h in range(NH):
                for j in range(NJ):
                    pieces = att_pieces(j)
                    for pi2, piece in enumerate(pieces):
                        plan.append(dict(gi=len(plan), h=h, j=j, pi2=pi2, piece=piece,
                                         first=(pi2 == 0), last=(pi2 == len(pieces) - 1),
                                         hj=h * NJ + j, npieces=len(pieces)))
            NPC = len(plan)
            last_gi_of_head = {}
            for p in plan:
                last_gi_of_head[p["h"]] = p["gi"]

            lk0, pm0, ae0, ve0, st0 = LK.n, PM.n, AE.n, VE.n, ST.n

            pm_scores = [0] * NPC; pm_pt = [0] * NPC; pm_av = [0] * NPC
            pm_vt = {}
            ae_exp = [ae0 + gi + 1 for gi in range(NPC)]
            ve_ptevac = [0] * NPC; ve_oevac = {}; st_orow = {}
            pst_free = {0: 0, 1: 0}
            pe_ops = []
            dve_ops = []
            pmv, vev, stv = pm0, ve0, st0

            def pe_iter():
                for h in range(NH):
                    yield ("ld", h)
                    for grp in range(4):
                        yield ("vt", h, grp)
                    for p in plan:
                        if p["h"] != h:
                            continue
                        gi = p["gi"]
                        yield ("sc", gi)
                        if gi >= 2:
                            yield ("pt", gi - 2)
                        if gi >= 3:
                            yield ("av", gi - 3)
                yield ("pt", NPC - 2)
                yield ("pt", NPC - 1)
                yield ("av", NPC - 3)
                yield ("av", NPC - 2)
                yield ("av", NPC - 1)

            for ev in pe_iter():
                kind = ev[0]
                if kind == "ld":
                    continue
                if kind == "vt":
                    _, h, grp = ev
                    half = grp % 2
                    w = [("LK", lk0 + (h + 1) * 9 * 16)] if grp == 0 else []
                    if pst_free[half]:
                        w.append(("VE", pst_free[half]))
                    pmv += 1
                    pm_vt[(h, grp)] = pmv
                    pe_ops.append(("vt", (h, grp), w))
                    dw = [("PM", pmv)]
                    if h >= 2:
                        dw.append(("PM", pm_av[last_gi_of_head[h - 2]]))
                    vev += 1
                    pst_free[half] = vev
                    dve_ops.append(("vtevac", (h, grp), dw, vev))
                elif kind == "sc":
                    gi = ev[1]
                    w = []
                    if gi >= 2:
                        w.append(("AE", ae_exp[gi - 2]))
                    pmv += 1
                    pm_scores[gi] = pmv
                    pe_ops.append(("sc", gi, w))
                elif kind == "pt":
                    gi = ev[1]
                    half = gi % 2
                    w = [("AE", ae_exp[gi])]
                    if pst_free[half]:
                        w.append(("VE", pst_free[half]))
                    pmv += 1
                    pm_pt[gi] = pmv
                    pe_ops.append(("pt", gi, w))
                    vev += 1
                    ve_ptevac[gi] = vev
                    pst_free[half] = vev
                    dve_ops.append(("ptevac", gi, [("PM", pmv)], vev))
                elif kind == "av":
                    gi = ev[1]
                    p = plan[gi]
                    w = [("VE", ve_ptevac[gi])]
                    if p["first"] and p["hj"] >= 1:
                        w.append(("VE", ve_oevac[p["hj"] - 1]))
                    pmv += 1
                    pm_av[gi] = pmv
                    pe_ops.append(("av", gi, w))
                    if p["last"]:
                        dw = [("PM", pmv), ("AE", ae_exp[gi])]
                        h = p["h"]
                        if p["j"] == 0 and h >= 2:
                            dw.append(("ST", st_orow[h - 2]))
                        vev += 1
                        ve_oevac[p["hj"]] = vev
                        dve_ops.append(("omerge", p["hj"], dw, vev))
                        if p["j"] == NJ - 1:
                            stv += 16
                            st_orow[h] = stv

            def f_t(t):
                for op in pe_ops:
                    kind, key, w = op
                    for (cn, val) in w:
                        if cn == "LK":
                            Wt(t, LK, val)
                        elif cn == "AE":
                            Wt(t, AE, val)
                        elif cn == "VE":
                            Wt(t, VE, val)
                    if kind == "vt":
                        h, grp = key
                        half = grp % 2
                        mm = None
                        for cc2 in range(8):
                            ci = grp * 8 + cc2
                            mm = t.transpose(ps_T[:, half * 1024 + cc2 * 128:half * 1024 + (cc2 + 1) * 128],
                                             vsb[:, ci * 128:(ci + 1) * 128], id_sb[:])
                        PM.inc(mm)
                    elif kind == "sc":
                        gi = key
                        p = plan[gi]
                        h, j, par = p["h"], p["j"], gi % 2
                        col = 0
                        mm = None
                        first_banks = set()
                        for (rp, i0, i1) in runs_of(p["piece"]):
                            while i0 < i1:
                                room = 4 - (col // 128) % 4
                                i1c = min(i0 + room, i1)
                                n = (i1c - i0) * 128
                                bank = col // 512
                                st = bank not in first_banks
                                first_banks.add(bank)
                                mm = t.matmul(ps_S[:, par * 1024 + col:par * 1024 + col + n],
                                              qsb[h % 2][:, j * 128:(j + 1) * 128],
                                              ksb[h % 2][:, rp * T + i0 * 128:rp * T + i1c * 128],
                                              start=st, stop=not (p["first"] and col < 512),
                                              skip_group_check=True)
                                col += n
                                i0 = i1c
                        if p["first"]:
                            mm = t.matmul(ps_S[:, par * 1024:par * 1024 + 512],
                                          id_sb[:], bmask_sb[:],
                                          start=False, stop=True, skip_group_check=True)
                        PM.inc(mm)
                    elif kind == "pt":
                        gi = key
                        p = plan[gi]
                        half = gi % 2
                        mm = None
                        for c in range(len(p["piece"])):
                            mm = t.transpose(ps_T[:, half * 1024 + c * 128:half * 1024 + (c + 1) * 128],
                                             pbuf[gi % 2][:, c * 128:(c + 1) * 128], id_sb[:])
                        PM.inc(mm)
                    elif kind == "av":
                        gi = key
                        p = plan[gi]
                        h = p["h"]
                        mm = None
                        nch = len(p["piece"])
                        for c, (rp, i) in enumerate(p["piece"]):
                            mm = t.matmul(ps_O[:, 0:128],
                                          ptbuf[gi % 2][:, c * 128:(c + 1) * 128],
                                          vt_sb[h % 2][:, (rp * NJ + i) * 128:(rp * NJ + i + 1) * 128],
                                          start=(p["first"] and c == 0),
                                          stop=(p["last"] and c == nch - 1),
                                          skip_group_check=True)
                        PM.inc(mm)
            section("tensor", f_t)
            assert PM.n == pmv, (PM.n, pmv)

            def f_sc(sc):
                for gi in range(NPC):
                    p = plan[gi]
                    ncols = len(p["piece"]) * 128
                    par = gi % 2
                    trig = pm_scores[gi] if gi < 2 else max(pm_scores[gi], pm_pt[gi - 2])
                    Wt(sc, PM, trig)
                    if p["hj"] >= 2 and p["first"]:
                        Wt(sc, VE, ve_oevac[p["hj"] - 2])
                    col = (p["hj"] % 2) * 4 + p["pi2"]
                    AE.inc(sc.activation(pbuf[par][:, 0:ncols],
                                         ps_S[:, par * 1024:par * 1024 + ncols], AF.Exp,
                                         bias=negb[:, p["h"]:p["h"] + 1],
                                         accum_out=dacc[:, col:col + 1]))
            section("scalar", f_sc)
            assert AE.n == ae_exp[-1]

            def f_v(v):
                for op in dve_ops:
                    kind, key, dw, vtarget = op
                    for (cn, val) in dw:
                        if cn == "PM":
                            Wt(v, PM, val)
                        elif cn == "AE":
                            Wt(v, AE, val)
                        elif cn == "ST":
                            Wt(v, ST, val)
                    if kind == "vtevac":
                        h, grp = key
                        half = grp % 2
                        VE.inc(v.tensor_copy(vt_sb[h % 2][:, grp * 1024:(grp + 1) * 1024],
                                             ps_T[:, half * 1024:(half + 1) * 1024]))
                    elif kind == "ptevac":
                        gi = key
                        ncols = len(plan[gi]["piece"]) * 128
                        half = gi % 2
                        VE.inc(v.tensor_copy(ptbuf[gi % 2][:, 0:ncols],
                                             ps_T[:, half * 1024:half * 1024 + ncols]))
                    elif kind == "omerge":
                        hj = key
                        h, j = divmod(hj, NJ)
                        npieces = len(att_pieces(j))
                        base = (hj % 2) * 4
                        selfsync(v, v.tensor_reduce(dtot[:], dacc[:, base:base + npieces], AX, ALU.add))
                        selfsync(v, v.reciprocal(drec[:], dtot[:]))
                        VE.inc(v.tensor_scalar(orow[h % 2][:, j * 128:(j + 1) * 128],
                                               ps_O[:, 0:128], drec[:], None, ALU.mult))
                    assert VE.n == vtarget
            section("vector", f_v)
            assert VE.n == vev, (VE.n, vev)

            def store_orow(s, hs):
                Wt(s, VE, ve_oevac[hs * NJ + NJ - 1])
                s.dma_start(out=abuf[:, hs * T:(hs + 1) * T].rearrange("p (c q) -> p c q", c=NJ),
                            in_=orow[hs % 2][:].rearrange("p (c q) -> p c q", c=NJ),
                            transpose=True).then_inc(ST.sem, 16)

            def f_s(s):
                Wt(s, ST, st_qkv)
                for h in range(NH):
                    if h >= 2:
                        Wt(s, PM, pm_scores[last_gi_of_head[h - 2]])
                    for rp in range(NR):
                        s.dma_start(out=ksb[h % 2][:, rp * T:(rp + 1) * T],
                                    in_=kv_all[rp * KVROWS + h * 128:rp * KVROWS + (h + 1) * 128, :]).then_inc(LK.sem, 16)
                    if h >= 1:
                        Wt(s, PM, pm_vt[(h - 1, 3)])
                    for rp in range(NR):
                        s.dma_start(out=vsb[:, rp * T:(rp + 1) * T],
                                    in_=kv_all[rp * KVROWS + NH * 128 + h * 128:rp * KVROWS + NH * 128 + (h + 1) * 128, :]).then_inc(LK.sem, 16)
                    s.dma_start(out=qsb[h % 2], in_=q_d[h]).then_inc(LK.sem, 16)
                    if h >= 2:
                        store_orow(s, h - 2)
                for hs in (NH - 2, NH - 1):
                    store_orow(s, hs)
                s.wait_ge(ST.sem, stv)
            section("sync", f_s)
            LK.n += NH * 9 * 16
            ST.n += NH * 16
            assert ST.n == stv and LK.n == lk0 + NH * 9 * 16
            waits.append((ST, stv))

            if dump_to is not None:
                def f_dp(s):
                    s.dma_start(out=dump_to, in_=abuf[:]).then_inc(ST.sem, 16)
                section("sync", f_dp)
                ST.n += 16
            barrier()

        st_after_x1 = st_after_qkv = st_after_x2 = 0
        if upto >= 1:
            ln_phase(x_in, dump_to=dumps.get("u1"))
        if upto >= 2:
            ffn_rl()
        if upto >= 3:
            proj_res_sweep("wb", 4 * KC, 0.5, True, x_in, x1_d, dump_to=dumps.get("x1"))
            st_after_x1 = ST.n
        if upto >= 4:
            ln_phase(x1_d, st_ready=st_after_x1, dump_to=dumps.get("y"))
        if upto >= 5:
            qkv_phase(dump_to=dumps.get("qkv"))
            st_after_qkv = ST.n
        if upto >= 6:
            attention_phase(st_after_qkv, dump_to=dumps.get("o"))
        if upto >= 7:
            proj_res_sweep("wo", 3 * KC, 1.0, False, x1_d, x2_d,
                           st_ready=st_after_x1, dump_to=dumps.get("x2"))
            st_after_x2 = ST.n
        if upto >= 8:
            ln_phase(x2_d, st_ready=st_after_x2)
        if upto >= 9:
            ffn_rl()
        if upto >= 10:
            proj_res_sweep("wb", 4 * KC, 0.5, True, x2_d, out_ext, st_ready=st_after_x2)

        def f_final(s):
            Wt(s, ST, ST.n)
        section("sync", f_final)

        for (ctr, val) in waits:
            assert val <= ctr.n, f"wait {ctr.name} {val} exceeds total {ctr.n}"
    return nc


def _stream_weight(Wm):
    return np.ascontiguousarray(
        np.asarray(Wm, dtype=np.float32).reshape(KC, 128, KC, 128).transpose(3, 0, 2, 1))


def _prep_shared(inputs):
    sh = {}
    for nm, key in (("wq", "L0_attn_Wq"), ("wk", "L0_attn_Wk"),
                    ("wv", "L0_attn_Wv"), ("wo", "L0_attn_Wo"),
                    ("wb", "L0_ffn_W_base")):
        sh[nm + "_s"] = _stream_weight(inputs[key])
    Wr = np.asarray(inputs["L0_ffn_W_router"], dtype=np.float32)
    la = np.asarray(inputs["L0_ffn_lora_A"], dtype=np.float32)
    rpart = np.ascontiguousarray(Wr.T).reshape(KC, 128, E).transpose(1, 0, 2)
    lpart = la.transpose(1, 0, 2).reshape(D, E * RL).reshape(KC, 128, E * RL).transpose(1, 0, 2)
    sh["rcat_s"] = np.ascontiguousarray(
        np.concatenate([rpart, lpart], axis=2), dtype=np.float32)
    lb = np.asarray(inputs["L0_ffn_lora_B"], dtype=np.float32).reshape(E * RL, D)
    sh["lb_s"] = np.ascontiguousarray(lb.reshape(128, KC, 128))
    bias = np.zeros((128, 6, KC), dtype=np.float32)
    for a, key in enumerate(("L0_attn_bq", "L0_attn_bk", "L0_attn_bv",
                             "L0_attn_bo", "L0_ffn_b_base")):
        bias[:, a, :] = np.asarray(inputs[key], dtype=np.float32).reshape(KC, 128).T
    sh["bias_s"] = bias
    sh["brl_s"] = np.ascontiguousarray(np.tile(
        np.asarray(inputs["L0_ffn_b_router"], dtype=np.float32).reshape(1, E), (128, 1)))
    sh["id_s"] = np.eye(128, dtype=np.float32)
    sh["ones_s"] = np.ones((128, 128), dtype=np.float32)
    return sh


def _core_x(x, c):
    g, r = divmod(c, 4)
    xs = np.asarray(x[g], dtype=np.float32).reshape(KC, 128, D)[r::4]
    tokmaj = xs.reshape(T, D)
    return np.ascontiguousarray(tokmaj.T.reshape(KC, 128, T).transpose(1, 0, 2))


def _core_bmask(c):
    r = c % 4
    m = np.zeros((128, 512), dtype=np.float32)
    qq = np.arange(128)[:, None]
    kk = np.arange(128)[None, :]
    for rp in range(NR):
        if rp == r:
            m[:, rp * 128:(rp + 1) * 128] = np.where(kk <= qq, 0.0, NEG)
        elif rp > r:
            m[:, rp * 128:(rp + 1) * 128] = NEG
    return m


def _in_maps(inputs):
    sh = _prep_shared(inputs)
    x = np.asarray(inputs["L0_attn_x"], dtype=np.float32)
    maps = []
    for c in range(8):
        m = dict(sh)
        m["x_s"] = _core_x(x, c)
        m["bmask_s"] = _core_bmask(c)
        maps.append(m)
    return maps


def _assemble(results):
    out = np.empty((2, 4096, D), dtype=np.float32)
    for c in range(8):
        g, r = divmod(c, 4)
        o = np.asarray(results[c]["out_s"])
        tokmaj = o.transpose(2, 1, 0).reshape(T, D)
        blocks = tokmaj.reshape(NJ, 128, D)
        for j in range(NJ):
            out[g, (4 * j + r) * 128:(4 * j + r + 1) * 128, :] = blocks[j]
    return out


def kernel(**inputs):
    nc = build_program()
    res = run_bass_kernel_spmd(nc, _in_maps(inputs), core_ids=list(range(8)))
    return _assemble(res.results)


if __name__ == "__main__":
    build_program()
    print("build ok")


# version 02
# speedup vs baseline: 1.0765x; 1.0765x over previous
import math
import numpy as np
import concourse.bass as bass
import concourse.mybir as mybir
from concourse.bass_utils import run_bass_kernel_spmd

F32 = mybir.dt.float32
BF16 = mybir.dt.bfloat16
AF = mybir.ActivationFunctionType
ALU = mybir.AluOpType
AX = mybir.AxisListType.X

D = 4096
KC = 32
T = 1024
NH = 32
NJ = 8
NR = 4
E = 8
RL = 16
EPS = 1e-5
ISQ = 1.0 / math.sqrt(128.0)
NEG = -1.0e30
KVROWS = 2 * NH * 128 + 128


def att_tiles(j):
    tiles = [(rp, j) for rp in range(NR)]
    for rp in range(NR):
        tiles += [(rp, i) for i in range(j)]
    return tiles


def att_pieces(j):
    t = att_tiles(j)
    return [t[k:k + 8] for k in range(0, len(t), 8)]


def runs_of(piece):
    runs = []
    for (rp, i) in piece:
        if runs and runs[-1][0] == rp and runs[-1][2] == i:
            runs[-1][2] += 1
        else:
            runs.append([rp, i, i + 1])
    return runs


class Ctr:
    def __init__(self, sem, name):
        self.sem = sem
        self.name = name
        self.n = 0
    def inc(self, inst, k=1):
        inst.then_inc(self.sem, k)
        self.n += k
        return self.n


def build_program(dump=False, upto=99):
    nc = bass.Bass("TRN2", target_bir_lowering=False, debug=False)

    x_in = nc.dram_tensor("x_s", [128, KC, T], F32, kind="ExternalInput").ap()
    w_in = {}
    for nm in ("wq", "wk", "wv", "wo", "wb"):
        w_in[nm] = nc.dram_tensor(nm + "_s", [128, KC, KC, 128], F32, kind="ExternalInput").ap()
    rcat_in = nc.dram_tensor("rcat_s", [128, KC, 136], F32, kind="ExternalInput").ap()
    lb_in = nc.dram_tensor("lb_s", [128, KC, 128], F32, kind="ExternalInput").ap()
    bias_in = nc.dram_tensor("bias_s", [128, 6, KC], F32, kind="ExternalInput").ap()
    brl_in = nc.dram_tensor("brl_s", [128, E], F32, kind="ExternalInput").ap()
    id_in = nc.dram_tensor("id_s", [128, 128], F32, kind="ExternalInput").ap()
    ones_in = nc.dram_tensor("ones_s", [128, 128], F32, kind="ExternalInput").ap()
    bmask_in = nc.dram_tensor("bmask_s", [128, 512], F32, kind="ExternalInput").ap()
    out_ext = nc.dram_tensor("out_s", [128, KC, T], F32, kind="ExternalOutput").ap()

    x1_d = nc.dram_tensor("x1_d", [128, KC, T], F32).ap()
    x2_d = nc.dram_tensor("x2_d", [128, KC, T], F32).ap()
    q_d = nc.dram_tensor("q_d", [NH, 128, T], BF16).ap()
    KVA = 2 * 16 * 128 + 128
    KVB = 2 * 16 * 128
    kvA = nc.dram_tensor("kvA", [KVA, T], BF16).ap()
    kvB = nc.dram_tensor("kvB", [KVB, T], BF16).ap()
    kvA_all = nc.dram_tensor("kvA_all", [NR * KVA, T], BF16).ap()
    kvB_all = nc.dram_tensor("kvB_all", [NR * KVB, T], BF16).ap()
    dumps = {}
    if dump:
        dumps["u1"] = nc.dram_tensor("dump_u1", [128, KC * T], BF16, kind="ExternalOutput").ap()
        dumps["x1"] = nc.dram_tensor("dump_x1", [128, KC, T], F32, kind="ExternalOutput").ap()
        dumps["y"] = nc.dram_tensor("dump_y", [128, KC * T], BF16, kind="ExternalOutput").ap()
        dumps["qkv"] = nc.dram_tensor("dump_qkv", [KVROWS + NH * 128, T], BF16, kind="ExternalOutput").ap()
        dumps["o"] = nc.dram_tensor("dump_o", [128, KC * T], BF16, kind="ExternalOutput").ap()
        dumps["x2"] = nc.dram_tensor("dump_x2", [128, KC, T], F32, kind="ExternalOutput").ap()

    import contextlib

    class DBuf:
        def __init__(self, t, n):
            self.t = t
            self.n = n
        def __getitem__(self, i):
            return self.t[:, i * self.n:(i + 1) * self.n]

    with contextlib.ExitStack() as stack:
        def sb(name, shape, dt):
            if len(shape) == 3:
                assert shape[0] == 2
                t = stack.enter_context(nc.sbuf_tensor(name, [shape[1], 2 * shape[2]], dt))
                return DBuf(t, shape[2])
            return stack.enter_context(nc.sbuf_tensor(name, shape, dt))
        def psb(name, shape, dt):
            return stack.enter_context(nc.psum_tensor(name, shape, dt))
        id_sb = sb("id_sb", [128, 128], BF16)
        ones_sb = sb("ones_sb", [128, 128], BF16)
        bmask_sb = sb("bmask_sb", [128, 512], BF16)
        rcat_b = sb("rcat_b", [128, KC * 136], BF16)
        lb_b = sb("lb_b", [128, KC * 128], BF16)
        bias_sb = sb("bias_sb", [128, 6 * KC], F32)
        biasx_sb = sb("biasx_sb", [128, 6 * KC], F32)
        brl_b = sb("brl_b", [128, E], F32)
        abuf = sb("abuf", [128, KC * T], BF16)
        mb_sb = sb("mb_sb", [128, T], F32)
        rstd_sb = sb("rstd_sb", [128, T], F32)
        tst_sb = sb("tst_sb", [128, T], F32)
        sqst = sb("sqst", [2, 128, T], BF16)
        xst = sb("xst", [2, 128, T], F32)
        estage = sb("estage", [2, 128, 512], F32)
        x1stage = sb("x1stage", [2, 128, T], F32)
        wslab = sb("wslab", [2, 128, KC * 128], BF16)
        pstage = sb("pstage", [2, 128, T], BF16)
        lgt = sb("lgt", [128, E], F32)
        rot = sb("rot", [128, E], F32)
        gex = sb("gex", [128, E], F32)
        tA = sb("tA", [128, E], F32)
        accn = sb("accn", [128, E], F32)
        accs = sb("accs", [128, E], F32)
        kz = sb("kz", [128, 1], F32)
        ssum = sb("ssum", [128, 1], F32)
        tau = sb("tau", [128, 1], F32)
        wsp = sb("wsp", [128, E], F32)
        rots = sb("rots", [128, 7 * E], F32)
        ges = sb("ges", [128, 7 * E], F32)
        gem = sb("gem", [128, 7 * E], F32)
        nred = sb("nred", [128, E], F32)
        sred = sb("sred", [128, E], F32)
        q1s = sb("q1s", [128, E], F32)
        q2s = sb("q2s", [128, E], F32)
        inds = sb("inds", [128, E], F32)
        szz = sb("szz", [128, E], F32)
        s2s = sb("s2s", [128, 1], F32)
        t1s = sb("t1s", [128, 1], F32)
        rks = sb("rks", [128, 1], F32)
        bm1 = sb("bm1", [128, NH], F32)
        bm2 = sb("bm2", [128, NH], F32)
        bm3 = sb("bm3", [128, NH], F32)
        negb2 = sb("negb2", [128, NH], F32)
        hw_sb = sb("hw_sb", [128, 128], BF16)
        hwT_sb = sb("hwT_sb", [128, T], BF16)
        qn2 = sb("qn2", [128, NH], F32)
        kn2 = sb("kn2", [128, NH], F32)
        kn2g = sb("kn2g", [128, NR * NH], F32)
        rmx = sb("rmx", [128, 1], F32)
        negb = sb("negb", [128, NH], F32)
        ksb = sb("ksb", [2, 128, NR * T], BF16)
        vsb = sb("vsb", [128, NR * T], BF16)
        qsb = sb("qsb", [2, 128, T], BF16)
        vt_sb = sb("vt_sb", [2, 128, NR * T], BF16)
        pbuf = sb("pbuf", [2, 128, T], BF16)
        ptbuf = sb("ptbuf", [2, 128, T], BF16)
        orow = sb("orow", [2, 128, T], BF16)
        dacc = sb("dacc", [128, 8], F32)
        dtot = sb("dtot", [128, 1], F32)
        drec = sb("drec", [128, 1], F32)
        ps_S = psb("ps_S", [128, 2048], F32)
        ps_O = psb("ps_O", [128, 512], F32)
        ps_T = psb("ps_T", [128, 2048], BF16)
        ps_M = psb("ps_M", [128, 512], F32)
        _sLW = stack.enter_context(nc.semaphore("sLW"))
        _sLX = stack.enter_context(nc.semaphore("sLX"))
        _sLK = stack.enter_context(nc.semaphore("sLK"))
        _sST = stack.enter_context(nc.semaphore("sST"))
        _sCC = stack.enter_context(nc.semaphore("sCC"))
        _sPM = stack.enter_context(nc.semaphore("sPM"))
        _sPT = stack.enter_context(nc.semaphore("sPT"))
        _sAE = stack.enter_context(nc.semaphore("sAE"))
        _sVE = stack.enter_context(nc.semaphore("sVE"))
        _sPH = stack.enter_context(nc.semaphore("sPH"))
        _sDV = stack.enter_context(nc.semaphore("sDV"))
        _sDS = stack.enter_context(nc.semaphore("sDS"))
        block = stack.enter_context(nc.Block())
        LW = Ctr(_sLW, "LW"); LX = Ctr(_sLX, "LX"); LK = Ctr(_sLK, "LK")
        ST = Ctr(_sST, "ST"); CC = Ctr(_sCC, "CC"); PM = Ctr(_sPM, "PM")
        PT = Ctr(_sPT, "PT"); AE = Ctr(_sAE, "AE"); VE = Ctr(_sVE, "VE")
        PH = Ctr(_sPH, "PH")
        DV = Ctr(_sDV, "DV")
        DS = Ctr(_sDS, "DS")

        def selfsync(eng, inst, ctr=None):
            c = ctr if ctr is not None else DV
            c.inc(inst)
            eng.wait_ge(c.sem, c.n)
        waits = []

        def Wt(eng, ctr, val):
            if val > 0:
                eng.wait_ge(ctr.sem, val)
                waits.append((ctr, val))

        def section(name, fn):
            dec = getattr(block, name)
            @dec
            def _(eng):
                fn(eng)

        def barrier():
            target = PH.n + 5
            for name in ("sync", "gpsimd", "tensor", "scalar", "vector"):
                def fn(eng, target=target):
                    eng.sem_inc(PH.sem, 1)
                    eng.wait_ge(PH.sem, target)
                section(name, fn)
            PH.n = target
            waits.append((PH, target))

        def p0_gp(g):
            g.dma_start(out=id_sb[:], in_=id_in).then_inc(LW.sem, 16)
            g.dma_start(out=ones_sb[:], in_=ones_in).then_inc(LW.sem, 16)
            g.dma_start(out=bmask_sb[:], in_=bmask_in).then_inc(LW.sem, 16)
            g.dma_start(out=rcat_b[:], in_=rcat_in.rearrange("p k c -> p (k c)")).then_inc(LW.sem, 16)
            g.dma_start(out=lb_b[:], in_=lb_in.rearrange("p k c -> p (k c)")).then_inc(LW.sem, 16)
        section("gpsimd", p0_gp)
        LW.n += 5 * 16

        def p0_sync(s):
            s.dma_start(out=bias_sb[:], in_=bias_in.rearrange("p a k -> p (a k)")).then_inc(LX.sem, 16)
            s.dma_start(out=brl_b[:], in_=brl_in).then_inc(LX.sem, 16)
        section("sync", p0_sync)
        LX.n += 32

        def p0_v(v):
            Wt(v, LX, 32)
            v.tensor_scalar(biasx_sb[:, 0:KC], bias_sb[:, 0:KC], ISQ, None, ALU.mult)
            v.tensor_copy(biasx_sb[:, KC:4 * KC], bias_sb[:, KC:4 * KC])
            v.tensor_scalar(biasx_sb[:, 4 * KC:5 * KC], bias_sb[:, 4 * KC:5 * KC], 0.5, None, ALU.mult)
            VE.inc(v.memset(biasx_sb[:, 5 * KC:6 * KC], 0.0))
        section("vector", p0_v)

        def p0_t(t):
            Wt(t, LW, 5 * 16)
            Wt(t, VE, VE.n)
        section("tensor", p0_t)
        barrier()

        def ln_phase(src_ap, st_ready=0, dump_to=None):
            lx0, pm0, ae0, ve0 = LX.n, PM.n, AE.n, VE.n

            def f_sync(s):
                Wt(s, ST, st_ready)
                for kc in range(KC):
                    if kc >= 2:
                        Wt(s, AE, ae0 + kc - 1)
                    s.dma_start(out=xst[kc % 2], in_=src_ap[:, kc, :]).then_inc(LX.sem, 16)
            section("sync", f_sync)
            LX.n += KC * 16

            def f_sc(sc):
                for kc in range(KC):
                    Wt(sc, LX, lx0 + (kc + 1) * 16)
                    AE.inc(sc.copy(abuf[:, kc * T:(kc + 1) * T], xst[kc % 2]))
            section("scalar", f_sc)

            def f_v(v):
                for kc in range(KC):
                    Wt(v, AE, ae0 + kc + 1)
                    if kc >= 2:
                        Wt(v, PM, pm0 + 4 * (kc - 1))
                    zb = abuf[:, kc * T:(kc + 1) * T]
                    VE.inc(v.tensor_tensor(sqst[kc % 2], zb, zb, ALU.mult))
            section("vector", f_v)

            def f_t(t):
                for kc in range(KC):
                    Wt(t, VE, ve0 + kc + 1)
                    zb = abuf[:, kc * T:(kc + 1) * T]
                    mm = None
                    for half in range(2):
                        t.matmul(ps_S[:, half * 512:(half + 1) * 512],
                                 ones_sb[:], zb[:, half * 512:(half + 1) * 512],
                                 start=(kc == 0), stop=(kc == KC - 1))
                        mm = t.matmul(ps_S[:, 1024 + half * 512:1024 + (half + 1) * 512],
                                      ones_sb[:], sqst[kc % 2][:, half * 512:(half + 1) * 512],
                                      start=(kc == 0), stop=(kc == KC - 1))
                    PM.inc(mm, 4)
            section("tensor", f_t)
            pm_stats = pm0 + 4 * KC

            def f_v2(v):
                Wt(v, PM, pm_stats)
                v.tensor_scalar(mb_sb[:], ps_S[:, 0:1024], 1.0 / D, None, ALU.mult)
                v.tensor_scalar(tst_sb[:], ps_S[:, 1024:2048], 1.0 / D, None, ALU.mult)
                v.tensor_tensor(rstd_sb[:], mb_sb[:], mb_sb[:], ALU.mult)
                v.tensor_tensor(tst_sb[:], tst_sb[:], rstd_sb[:], ALU.subtract)
                v.tensor_scalar(tst_sb[:], tst_sb[:], EPS, None, ALU.add)
                VE.inc(v.reciprocal(rstd_sb[:], tst_sb[:]))
            section("vector", f_v2)

            def f_sc2(sc):
                Wt(sc, VE, VE.n)
                AE.inc(sc.activation(rstd_sb[:], rstd_sb[:], AF.Sqrt))
            section("scalar", f_sc2)

            def f_v3(v):
                Wt(v, AE, AE.n)
                mm = None
                for kc in range(KC):
                    zb = abuf[:, kc * T:(kc + 1) * T]
                    v.tensor_tensor(tst_sb[:], zb, mb_sb[:], ALU.subtract)
                    mm = v.tensor_tensor(zb, tst_sb[:], rstd_sb[:], ALU.mult)
                VE.inc(mm)
            section("vector", f_v3)

            if dump_to is not None:
                def f_dp(s):
                    Wt(s, VE, VE.n)
                    s.dma_start(out=dump_to, in_=abuf[:]).then_inc(ST.sem, 16)
                section("sync", f_dp)
                ST.n += 16
            barrier()

        def ffn_rl():
            pm0, ve0, pt0 = PM.n, VE.n, PT.n

            def f_t(t):
                for tt in range(NJ):
                    off = 1024 + (tt % 2) * 512
                    ps_rl = ps_S[:, off:off + 136]
                    if tt >= 2:
                        Wt(t, VE, ve0 + (tt - 1) * 10)
                    mm = None
                    for kc in range(KC):
                        mm = t.matmul(ps_rl, abuf[:, kc * T + tt * 128:kc * T + (tt + 1) * 128],
                                      rcat_b[:, kc * 136:(kc + 1) * 136],
                                      start=(kc == 0), stop=(kc == KC - 1))
                    PM.inc(mm)
                    Wt(t, VE, ve0 + tt * 10 + 9)
                    if tt >= 2:
                        Wt(t, VE, ve0 + (tt - 2) * 10 + 10)
                    PT.inc(t.transpose(ps_T[:, (tt % 2) * 1024:(tt % 2) * 1024 + 128],
                                       hw_sb[:], id_sb[:]))
            section("tensor", f_t)

            def f_v(v):
                for tt in range(NJ):
                    off = 1024 + (tt % 2) * 512
                    ps_rl = ps_S[:, off:off + 136]
                    Wt(v, PM, pm0 + tt + 1)
                    ges3 = ges[:].rearrange("p (e j) -> p e j", j=7)
                    gem3 = gem[:].rearrange("p (e j) -> p e j", j=7)
                    rots3 = rots[:].rearrange("p (j e) -> p j e", j=7)
                    selfsync(v, v.tensor_tensor(lgt[:], ps_rl[:, 0:E], brl_b[:], ALU.add))
                    mm = None
                    for jj in range(1, E):
                        v.tensor_copy(rots[:, (jj - 1) * E:(jj - 1) * E + E - jj], lgt[:, jj:E])
                        mm = v.tensor_copy(rots[:, (jj - 1) * E + E - jj:jj * E], lgt[:, 0:jj])
                    selfsync(v, mm)
                    for jj in range(1, E):
                        mm = v.tensor_tensor(ges3[:, :, jj - 1], rots3[:, jj - 1, :], lgt[:], ALU.is_ge)
                    selfsync(v, mm)
                    for jj in range(1, E):
                        mm = v.tensor_tensor(gem3[:, :, jj - 1], ges3[:, :, jj - 1],
                                             rots3[:, jj - 1, :], ALU.mult)
                    selfsync(v, mm)
                    v.tensor_reduce(nred[:], ges3, AX, ALU.add)
                    selfsync(v, v.tensor_reduce(sred[:], gem3, AX, ALU.add))
                    selfsync(v, v.tensor_tensor(q1s[:], nred[:], lgt[:], ALU.mult))
                    selfsync(v, v.scalar_tensor_tensor(q2s[:], q1s[:], 1.0, sred[:], ALU.add, ALU.subtract))
                    selfsync(v, v.tensor_single_scalar(inds[:], q2s[:], 0.0, ALU.is_gt))
                    v.tensor_reduce(kz[:], inds[:], AX, ALU.add)
                    selfsync(v, v.tensor_tensor(szz[:], inds[:], lgt[:], ALU.mult))
                    v.tensor_reduce(s2s[:], szz[:], AX, ALU.add)
                    selfsync(v, v.reciprocal(rks[:], kz[:]))
                    selfsync(v, v.tensor_scalar(t1s[:], s2s[:], -1.0, None, ALU.add))
                    selfsync(v, v.tensor_tensor(tau[:], t1s[:], rks[:], ALU.mult))
                    selfsync(v, v.tensor_scalar(wsp[:], lgt[:], tau[:], 0.0, ALU.subtract, ALU.max))
                    mm = None
                    for e in range(E):
                        mm = v.tensor_scalar(hw_sb[:, e * RL:(e + 1) * RL],
                                             ps_rl[:, 8 + e * RL:8 + (e + 1) * RL],
                                             wsp[:, e:e + 1], None, ALU.mult)
                    VE.inc(mm, 9)
                    Wt(v, PT, pt0 + tt + 1)
                    VE.inc(v.tensor_copy(hwT_sb[:, tt * 128:(tt + 1) * 128],
                                         ps_T[:, (tt % 2) * 1024:(tt % 2) * 1024 + 128]))
            section("vector", f_v)
            assert VE.n == ve0 + NJ * 10
            barrier()

        def proj_res_sweep(wname, bias_col0, scale, expert, res_ap, dst_ap,
                           st_ready=0, dump_to=None):
            lw0, lx0, pm0, ae0, ve0, st0 = LW.n, LX.n, PM.n, AE.n, VE.n, ST.n

            def f_gp(g):
                for mc in range(KC):
                    if mc >= 2:
                        Wt(g, PM, pm0 + mc - 1)
                    g.dma_start(out=wslab[mc % 2],
                                in_=w_in[wname][:, mc].rearrange("p k m -> p (k m)")).then_inc(LW.sem, 16)
            section("gpsimd", f_gp)
            LW.n += KC * 16

            def f_sync(s):
                Wt(s, ST, st_ready)
                for mc in range(KC):
                    if mc >= 2:
                        Wt(s, VE, ve0 + 2 * (mc - 1))
                    s.dma_start(out=xst[mc % 2], in_=res_ap[:, mc, :]).then_inc(LX.sem, 16)
                    if mc >= 1:
                        Wt(s, VE, ve0 + 2 * mc)
                        s.dma_start(out=dst_ap[:, mc - 1, :], in_=x1stage[(mc - 1) % 2]).then_inc(ST.sem, 16)
                Wt(s, VE, ve0 + 2 * KC)
                s.dma_start(out=dst_ap[:, KC - 1, :], in_=x1stage[(KC - 1) % 2]).then_inc(ST.sem, 16)
            section("sync", f_sync)
            LX.n += KC * 16
            ST.n += KC * 16

            def f_t(t):
                for mc in range(KC):
                    Wt(t, LW, lw0 + (mc + 1) * 16)
                    if mc >= 2:
                        Wt(t, AE, ae0 + 2 * (mc - 1))
                    base = (mc % 2) * 1024
                    mm = None
                    for kc in range(KC):
                        for tg in range(2):
                            mm = t.matmul(ps_S[:, base + tg * 512:base + tg * 512 + 512],
                                          wslab[mc % 2][:, kc * 128:(kc + 1) * 128],
                                          abuf[:, kc * T + tg * 512:kc * T + (tg + 1) * 512],
                                          start=(kc == 0), stop=(not expert and kc == KC - 1))
                    if expert:
                        for tg in range(2):
                            mm = t.matmul(ps_S[:, base + tg * 512:base + tg * 512 + 512],
                                          lb_b[:, mc * 128:(mc + 1) * 128],
                                          hwT_sb[:, tg * 512:(tg + 1) * 512],
                                          start=False, stop=True)
                    PM.inc(mm)
            section("tensor", f_t)

            def f_sc(sc):
                for mc in range(KC):
                    Wt(sc, PM, pm0 + mc + 1)
                    base = (mc % 2) * 1024
                    for tg in range(2):
                        gi = mc * 2 + tg
                        if gi >= 2:
                            Wt(sc, VE, ve0 + gi - 1)
                        ps = ps_S[:, base + tg * 512:base + tg * 512 + 512]
                        AE.inc(sc.activation(estage[gi % 2], ps, AF.Identity,
                                             bias=biasx_sb[:, bias_col0 + mc:bias_col0 + mc + 1],
                                             scale=scale))
            section("scalar", f_sc)

            def f_v(v):
                for mc in range(KC):
                    Wt(v, LX, lx0 + (mc + 1) * 16)
                    if mc >= 2:
                        Wt(v, ST, st0 + (mc - 1) * 16)
                    for tg in range(2):
                        gi = mc * 2 + tg
                        Wt(v, AE, ae0 + gi + 1)
                        VE.inc(v.tensor_tensor(x1stage[mc % 2][:, tg * 512:(tg + 1) * 512],
                                               estage[gi % 2],
                                               xst[mc % 2][:, tg * 512:(tg + 1) * 512], ALU.add))
            section("vector", f_v)

            if dump_to is not None:
                def f_dp(s):
                    Wt(s, ST, ST.n)
                    s.dma_start(out=dump_to.rearrange("p k t -> p (k t)"),
                                in_=dst_ap.rearrange("p k t -> p (k t)")).then_inc(ST.sem, 16)
                section("sync", f_dp)
                ST.n += 16
            barrier()

        def qkv_phase(dump_to=None):
            projs = ("wk", "wv", "wq")
            bias_base = (KC, 2 * KC, 0)
            lw0, pm0, pt0, ae0, ve0, st0, ccb = LW.n, PM.n, PT.n, AE.n, VE.n, ST.n, CC.n
            NSL = 3 * KC

            def nidx(sl):
                if sl is None or sl < 0:
                    return None
                if sl < KC:
                    return sl
                if sl >= 2 * KC:
                    return KC + (sl - 2 * KC)
                return None

            def prev_norm_slab(sl):
                p = sl - 2
                while p >= 0 and nidx(p) is None:
                    p -= 2
                return p if p >= 0 else None

            def f_gp(g):
                for sl in range(NSL):
                    pi, mc = divmod(sl, KC)
                    if sl == 50:
                        Wt(g, VE, ve0 + KC)
                        g.dma_start(out=kvA[2 * 16 * 128:2 * 16 * 128 + 128, 0:NH],
                                    in_=kn2[:]).then_inc(LK.sem, 16)
                        g.wait_ge(LK.sem, LK.n + 16)
                        Wt(g, ST, st0 + 48 * 16)
                        CC.inc(g.collective_compute(
                            "AllGather", ALU.bypass,
                            replica_groups=[[0, 1, 2, 3], [4, 5, 6, 7]],
                            ins=[kvA.opt()], outs=[kvA_all.opt()]))
                    if sl == 66:
                        g.wait_ge(CC.sem, ccb + 1)
                        Wt(g, ST, st0 + 64 * 16)
                        CC.inc(g.collective_compute(
                            "AllGather", ALU.bypass,
                            replica_groups=[[0, 1, 2, 3], [4, 5, 6, 7]],
                            ins=[kvB.opt()], outs=[kvB_all.opt()]))
                    if sl >= 2:
                        Wt(g, PM, pm0 + sl - 1)
                    g.dma_start(out=wslab[sl % 2],
                                in_=w_in[projs[pi]][:, mc].rearrange("p k m -> p (k m)")).then_inc(LW.sem, 16)
            section("gpsimd", f_gp)
            LW.n += NSL * 16
            LK.n += 16

            def norm_mms(t, snorm):
                ni = nidx(snorm)
                if ni is None:
                    return
                Wt(t, AE, ae0 + 4 * (snorm + 1))
                if ni >= 1:
                    Wt(t, VE, ve0 + ni)
                mm = None
                for tg in range(2):
                    mm = t.matmul(ps_M[:, 0:512], ones_sb[:],
                                  sqst[snorm % 2][:, tg * 512:(tg + 1) * 512],
                                  start=(tg == 0), stop=(tg == 1))
                PT.inc(mm)

            def f_t(t):
                for sl in range(NSL):
                    Wt(t, LW, lw0 + (sl + 1) * 16)
                    if sl >= 2:
                        Wt(t, AE, ae0 + 4 * (sl - 1))
                    base = (sl % 2) * 1024
                    mm = None
                    for kc in range(KC):
                        for tg in range(2):
                            mm = t.matmul(ps_S[:, base + tg * 512:base + tg * 512 + 512],
                                          wslab[sl % 2][:, kc * 128:(kc + 1) * 128],
                                          abuf[:, kc * T + tg * 512:kc * T + (tg + 1) * 512],
                                          start=(kc == 0), stop=(kc == KC - 1))
                    PM.inc(mm)
                    if sl >= 1:
                        norm_mms(t, sl - 1)
                norm_mms(t, NSL - 1)
            section("tensor", f_t)

            def f_sc(sc):
                for sl in range(NSL):
                    pi, mc = divmod(sl, KC)
                    scale = ISQ if pi == 2 else 1.0
                    Wt(sc, PM, pm0 + sl + 1)
                    base = (sl % 2) * 1024
                    for tg in range(2):
                        if sl >= 2 and tg == 0:
                            Wt(sc, ST, st0 + (sl - 1) * 16)
                        ps = ps_S[:, base + tg * 512:base + tg * 512 + 512]
                        last = sc.activation(pstage[sl % 2][:, tg * 512:(tg + 1) * 512], ps, AF.Identity,
                                             bias=biasx_sb[:, bias_base[pi] + mc:bias_base[pi] + mc + 1],
                                             scale=scale)
                        if pi != 1:
                            selfsync(sc, last, DS)
                            pns = prev_norm_slab(sl)
                            if pns is not None:
                                Wt(sc, PT, pt0 + nidx(pns) + 1)
                            last = sc.activation(sqst[sl % 2][:, tg * 512:(tg + 1) * 512],
                                                 pstage[sl % 2][:, tg * 512:(tg + 1) * 512], AF.Square)
                        AE.inc(last, 2)
            section("scalar", f_sc)

            def f_v(v):
                for ni in range(2 * KC):
                    Wt(v, PT, pt0 + ni + 1)
                    selfsync(v, v.tensor_reduce(rmx[:], ps_M[:, 0:512], AX, ALU.max))
                    dstn = kn2 if ni < KC else qn2
                    mc = ni if ni < KC else ni - KC
                    VE.inc(v.tensor_copy(dstn[:, mc:mc + 1], rmx[:]))
            section("vector", f_v)

            def f_sync(s):
                for sl in range(NSL):
                    pi, mc = divmod(sl, KC)
                    Wt(s, AE, ae0 + 4 * sl + 4)
                    if pi == 0:
                        dst = (kvA[mc * 128:(mc + 1) * 128, :] if mc < 16
                               else kvB[(mc - 16) * 128:(mc - 15) * 128, :])
                    elif pi == 1:
                        dst = (kvA[2048 + mc * 128:2048 + (mc + 1) * 128, :] if mc < 16
                               else kvB[2048 + (mc - 16) * 128:2048 + (mc - 15) * 128, :])
                    else:
                        dst = q_d[mc]
                    s.dma_start(out=dst, in_=pstage[sl % 2]).then_inc(ST.sem, 16)
            section("sync", f_sync)
            ST.n += NSL * 16

            if dump_to is not None:
                def f_dp(s):
                    Wt(s, ST, ST.n)
                    s.dma_start(out=dump_to[0:KVA, :], in_=kvA[:, :]).then_inc(ST.sem, 16)
                    s.dma_start(out=dump_to[KVA:KVA + KVB, :], in_=kvB[:, :]).then_inc(ST.sem, 16)
                    s.dma_start(out=dump_to[KVA + KVB:KVA + KVB + NH * 128, :],
                                in_=q_d.rearrange("h p t -> (h p) t")).then_inc(ST.sem, 16)
                section("sync", f_dp)
                ST.n += 48
            barrier()
            return ccb + 1, ccb + 2

        def attention_phase(st_qkv, cc0val, cc1val, dump_to=None):
            def f_g0(g):
                Wt(g, CC, cc0val)
                for rp in range(NR):
                    g.dma_start(out=kn2g[:, rp * NH:(rp + 1) * NH],
                                in_=kvA_all[rp * KVA + 2 * 16 * 128:rp * KVA + 2 * 16 * 128 + 128, 0:NH]).then_inc(LK.sem, 16)
            section("gpsimd", f_g0)
            LK.n += NR * 16

            def f_v0(v):
                Wt(v, LK, LK.n)
                v.tensor_tensor(bm1[:], kn2g[:, 0:NH], kn2g[:, NH:2 * NH], ALU.max)
                selfsync(v, v.tensor_tensor(bm2[:], kn2g[:, 2 * NH:3 * NH], kn2g[:, 3 * NH:4 * NH], ALU.max))
                selfsync(v, v.tensor_tensor(bm3[:], bm1[:], bm2[:], ALU.max))
                VE.inc(v.tensor_tensor(negb[:], bm3[:], qn2[:], ALU.mult))
            section("vector", f_v0)

            def f_sc0(sc):
                Wt(sc, VE, VE.n)
                selfsync(sc, sc.activation(negb2[:], negb[:], AF.Sqrt), DS)
                AE.inc(sc.mul(negb[:], negb2[:], -1.0))
            section("scalar", f_sc0)
            barrier()

            plan = []
            for h in range(NH):
                for j in range(NJ):
                    pieces = att_pieces(j)
                    for pi2, piece in enumerate(pieces):
                        plan.append(dict(gi=len(plan), h=h, j=j, pi2=pi2, piece=piece,
                                         first=(pi2 == 0), last=(pi2 == len(pieces) - 1),
                                         hj=h * NJ + j, npieces=len(pieces)))
            NPC = len(plan)
            last_gi_of_head = {}
            for p in plan:
                last_gi_of_head[p["h"]] = p["gi"]

            lk0, pm0, ae0, ve0, st0 = LK.n, PM.n, AE.n, VE.n, ST.n

            pm_scores = [0] * NPC; pm_pt = [0] * NPC; pm_av = [0] * NPC
            pm_vt = {}
            ae_exp = [ae0 + gi + 1 for gi in range(NPC)]
            ve_ptevac = [0] * NPC; ve_oevac = {}; st_orow = {}
            pst_free = {0: 0, 1: 0}
            pe_ops = []
            dve_ops = []
            pmv, vev, stv = pm0, ve0, st0

            def pe_iter():
                for h in range(NH):
                    yield ("ld", h)
                    for grp in range(4):
                        yield ("vt", h, grp)
                    for p in plan:
                        if p["h"] != h:
                            continue
                        gi = p["gi"]
                        yield ("sc", gi)
                        if gi >= 2:
                            yield ("pt", gi - 2)
                        if gi >= 3:
                            yield ("av", gi - 3)
                yield ("pt", NPC - 2)
                yield ("pt", NPC - 1)
                yield ("av", NPC - 3)
                yield ("av", NPC - 2)
                yield ("av", NPC - 1)

            for ev in pe_iter():
                kind = ev[0]
                if kind == "ld":
                    continue
                if kind == "vt":
                    _, h, grp = ev
                    half = grp % 2
                    w = [("LK", lk0 + (h + 1) * 9 * 16)] if grp == 0 else []
                    if pst_free[half]:
                        w.append(("VE", pst_free[half]))
                    pmv += 1
                    pm_vt[(h, grp)] = pmv
                    pe_ops.append(("vt", (h, grp), w))
                    dw = [("PM", pmv)]
                    if h >= 2:
                        dw.append(("PM", pm_av[last_gi_of_head[h - 2]]))
                    vev += 1
                    pst_free[half] = vev
                    dve_ops.append(("vtevac", (h, grp), dw, vev))
                elif kind == "sc":
                    gi = ev[1]
                    w = []
                    if gi >= 2:
                        w.append(("AE", ae_exp[gi - 2]))
                    pmv += 1
                    pm_scores[gi] = pmv
                    pe_ops.append(("sc", gi, w))
                elif kind == "pt":
                    gi = ev[1]
                    half = gi % 2
                    w = [("AE", ae_exp[gi])]
                    if pst_free[half]:
                        w.append(("VE", pst_free[half]))
                    pmv += 1
                    pm_pt[gi] = pmv
                    pe_ops.append(("pt", gi, w))
                    vev += 1
                    ve_ptevac[gi] = vev
                    pst_free[half] = vev
                    dve_ops.append(("ptevac", gi, [("PM", pmv)], vev))
                elif kind == "av":
                    gi = ev[1]
                    p = plan[gi]
                    w = [("VE", ve_ptevac[gi])]
                    if p["first"] and p["hj"] >= 1:
                        w.append(("VE", ve_oevac[p["hj"] - 1]))
                    pmv += 1
                    pm_av[gi] = pmv
                    pe_ops.append(("av", gi, w))
                    if p["last"]:
                        dw = [("PM", pmv), ("AE", ae_exp[gi])]
                        h = p["h"]
                        if p["j"] == 0 and h >= 2:
                            dw.append(("ST", st_orow[h - 2]))
                        vev += 1
                        ve_oevac[p["hj"]] = vev
                        dve_ops.append(("omerge", p["hj"], dw, vev))
                        if p["j"] == NJ - 1:
                            stv += 16
                            st_orow[h] = stv

            def f_t(t):
                for op in pe_ops:
                    kind, key, w = op
                    for (cn, val) in w:
                        if cn == "LK":
                            Wt(t, LK, val)
                        elif cn == "AE":
                            Wt(t, AE, val)
                        elif cn == "VE":
                            Wt(t, VE, val)
                    if kind == "vt":
                        h, grp = key
                        half = grp % 2
                        mm = None
                        for cc2 in range(8):
                            ci = grp * 8 + cc2
                            mm = t.transpose(ps_T[:, half * 1024 + cc2 * 128:half * 1024 + (cc2 + 1) * 128],
                                             vsb[:, ci * 128:(ci + 1) * 128], id_sb[:])
                        PM.inc(mm)
                    elif kind == "sc":
                        gi = key
                        p = plan[gi]
                        h, j, par = p["h"], p["j"], gi % 2
                        col = 0
                        mm = None
                        first_banks = set()
                        for (rp, i0, i1) in runs_of(p["piece"]):
                            while i0 < i1:
                                room = 4 - (col // 128) % 4
                                i1c = min(i0 + room, i1)
                                n = (i1c - i0) * 128
                                bank = col // 512
                                st = bank not in first_banks
                                first_banks.add(bank)
                                mm = t.matmul(ps_S[:, par * 1024 + col:par * 1024 + col + n],
                                              qsb[h % 2][:, j * 128:(j + 1) * 128],
                                              ksb[h % 2][:, rp * T + i0 * 128:rp * T + i1c * 128],
                                              start=st, stop=not (p["first"] and col < 512),
                                              skip_group_check=True)
                                col += n
                                i0 = i1c
                        if p["first"]:
                            mm = t.matmul(ps_S[:, par * 1024:par * 1024 + 512],
                                          id_sb[:], bmask_sb[:],
                                          start=False, stop=True, skip_group_check=True)
                        PM.inc(mm)
                    elif kind == "pt":
                        gi = key
                        p = plan[gi]
                        half = gi % 2
                        mm = None
                        for c in range(len(p["piece"])):
                            mm = t.transpose(ps_T[:, half * 1024 + c * 128:half * 1024 + (c + 1) * 128],
                                             pbuf[gi % 2][:, c * 128:(c + 1) * 128], id_sb[:])
                        PM.inc(mm)
                    elif kind == "av":
                        gi = key
                        p = plan[gi]
                        h = p["h"]
                        mm = None
                        nch = len(p["piece"])
                        for c, (rp, i) in enumerate(p["piece"]):
                            mm = t.matmul(ps_O[:, 0:128],
                                          ptbuf[gi % 2][:, c * 128:(c + 1) * 128],
                                          vt_sb[h % 2][:, (rp * NJ + i) * 128:(rp * NJ + i + 1) * 128],
                                          start=(p["first"] and c == 0),
                                          stop=(p["last"] and c == nch - 1),
                                          skip_group_check=True)
                        PM.inc(mm)
            section("tensor", f_t)
            assert PM.n == pmv, (PM.n, pmv)

            def f_sc(sc):
                for gi in range(NPC):
                    p = plan[gi]
                    ncols = len(p["piece"]) * 128
                    par = gi % 2
                    trig = pm_scores[gi] if gi < 2 else max(pm_scores[gi], pm_pt[gi - 2])
                    Wt(sc, PM, trig)
                    if p["hj"] >= 2 and p["first"]:
                        Wt(sc, VE, ve_oevac[p["hj"] - 2])
                    col = (p["hj"] % 2) * 4 + p["pi2"]
                    AE.inc(sc.activation(pbuf[par][:, 0:ncols],
                                         ps_S[:, par * 1024:par * 1024 + ncols], AF.Exp,
                                         bias=negb[:, p["h"]:p["h"] + 1],
                                         accum_out=dacc[:, col:col + 1]))
            section("scalar", f_sc)
            assert AE.n == ae_exp[-1]

            def f_v(v):
                for op in dve_ops:
                    kind, key, dw, vtarget = op
                    for (cn, val) in dw:
                        if cn == "PM":
                            Wt(v, PM, val)
                        elif cn == "AE":
                            Wt(v, AE, val)
                        elif cn == "ST":
                            Wt(v, ST, val)
                    if kind == "vtevac":
                        h, grp = key
                        half = grp % 2
                        VE.inc(v.tensor_copy(vt_sb[h % 2][:, grp * 1024:(grp + 1) * 1024],
                                             ps_T[:, half * 1024:(half + 1) * 1024]))
                    elif kind == "ptevac":
                        gi = key
                        ncols = len(plan[gi]["piece"]) * 128
                        half = gi % 2
                        VE.inc(v.tensor_copy(ptbuf[gi % 2][:, 0:ncols],
                                             ps_T[:, half * 1024:half * 1024 + ncols]))
                    elif kind == "omerge":
                        hj = key
                        h, j = divmod(hj, NJ)
                        npieces = len(att_pieces(j))
                        base = (hj % 2) * 4
                        selfsync(v, v.tensor_reduce(dtot[:], dacc[:, base:base + npieces], AX, ALU.add))
                        selfsync(v, v.reciprocal(drec[:], dtot[:]))
                        VE.inc(v.tensor_scalar(orow[h % 2][:, j * 128:(j + 1) * 128],
                                               ps_O[:, 0:128], drec[:], None, ALU.mult))
                    assert VE.n == vtarget
            section("vector", f_v)
            assert VE.n == vev, (VE.n, vev)

            def store_orow(s, hs):
                Wt(s, VE, ve_oevac[hs * NJ + NJ - 1])
                s.dma_start(out=abuf[:, hs * T:(hs + 1) * T].rearrange("p (c q) -> p c q", c=NJ),
                            in_=orow[hs % 2][:].rearrange("p (c q) -> p c q", c=NJ),
                            transpose=True).then_inc(ST.sem, 16)

            def f_s(s):
                Wt(s, ST, st_qkv)
                for h in range(NH):
                    if h == 0:
                        Wt(s, CC, cc0val)
                    if h == 16:
                        Wt(s, CC, cc1val)
                    if h >= 2:
                        Wt(s, PM, pm_scores[last_gi_of_head[h - 2]])
                    if h < 16:
                        kbase, vbase, kv, stride = h * 128, 2048 + h * 128, kvA_all, KVA
                    else:
                        kbase, vbase, kv, stride = (h - 16) * 128, 2048 + (h - 16) * 128, kvB_all, KVB
                    for rp in range(NR):
                        s.dma_start(out=ksb[h % 2][:, rp * T:(rp + 1) * T],
                                    in_=kv[rp * stride + kbase:rp * stride + kbase + 128, :]).then_inc(LK.sem, 16)
                    if h >= 1:
                        Wt(s, PM, pm_vt[(h - 1, 3)])
                    for rp in range(NR):
                        s.dma_start(out=vsb[:, rp * T:(rp + 1) * T],
                                    in_=kv[rp * stride + vbase:rp * stride + vbase + 128, :]).then_inc(LK.sem, 16)
                    s.dma_start(out=qsb[h % 2], in_=q_d[h]).then_inc(LK.sem, 16)
                    if h >= 2:
                        store_orow(s, h - 2)
                for hs in (NH - 2, NH - 1):
                    store_orow(s, hs)
                s.wait_ge(ST.sem, stv)
            section("sync", f_s)
            LK.n += NH * 9 * 16
            ST.n += NH * 16
            assert ST.n == stv and LK.n == lk0 + NH * 9 * 16
            waits.append((ST, stv))

            if dump_to is not None:
                def f_dp(s):
                    s.dma_start(out=dump_to, in_=abuf[:]).then_inc(ST.sem, 16)
                section("sync", f_dp)
                ST.n += 16
            barrier()

        st_after_x1 = st_after_qkv = st_after_x2 = 0
        if upto == -7:
            ln_phase(x_in)
            proj_res_sweep("wo", 3 * KC, 1.0, False, x_in, x1_d, dump_to=dumps.get("x1"))
            upto = 0
        if upto == -8:
            ln_phase(x_in)
            ffn_rl()
            proj_res_sweep("wb", 4 * KC, 0.5, True, x_in, x1_d, dump_to=dumps.get("x1"))
            ln_phase(x1_d, st_ready=ST.n)
            proj_res_sweep("wo", 3 * KC, 1.0, False, x1_d, x2_d, dump_to=dumps.get("x2"))
            upto = 0
        if upto >= 1:
            ln_phase(x_in, dump_to=dumps.get("u1"))
        if upto >= 2:
            ffn_rl()
        if upto >= 3:
            proj_res_sweep("wb", 4 * KC, 0.5, True, x_in, x1_d, dump_to=dumps.get("x1"))
            st_after_x1 = ST.n
        if upto >= 4:
            ln_phase(x1_d, st_ready=st_after_x1, dump_to=dumps.get("y"))
        if upto >= 5:
            cc0val, cc1val = qkv_phase(dump_to=dumps.get("qkv"))
            st_after_qkv = ST.n
        if upto >= 6:
            attention_phase(st_after_qkv, cc0val, cc1val, dump_to=dumps.get("o"))
        if upto >= 7:
            proj_res_sweep("wo", 3 * KC, 1.0, False, x1_d, x2_d,
                           st_ready=st_after_x1, dump_to=dumps.get("x2"))
            st_after_x2 = ST.n
        if upto >= 8:
            ln_phase(x2_d, st_ready=st_after_x2)
        if upto >= 9:
            ffn_rl()
        if upto >= 10:
            proj_res_sweep("wb", 4 * KC, 0.5, True, x2_d, out_ext, st_ready=st_after_x2)

        def f_final(s):
            Wt(s, ST, ST.n)
        section("sync", f_final)

        for (ctr, val) in waits:
            assert val <= ctr.n, f"wait {ctr.name} {val} exceeds total {ctr.n}"
    return nc


def _stream_weight(Wm):
    return np.ascontiguousarray(
        np.asarray(Wm, dtype=np.float32).reshape(KC, 128, KC, 128).transpose(3, 0, 2, 1))


def _prep_shared(inputs):
    sh = {}
    for nm, key in (("wq", "L0_attn_Wq"), ("wk", "L0_attn_Wk"),
                    ("wv", "L0_attn_Wv"), ("wo", "L0_attn_Wo"),
                    ("wb", "L0_ffn_W_base")):
        sh[nm + "_s"] = _stream_weight(inputs[key])
    Wr = np.asarray(inputs["L0_ffn_W_router"], dtype=np.float32)
    la = np.asarray(inputs["L0_ffn_lora_A"], dtype=np.float32)
    rpart = np.ascontiguousarray(Wr.T).reshape(KC, 128, E).transpose(1, 0, 2)
    lpart = la.transpose(1, 0, 2).reshape(D, E * RL).reshape(KC, 128, E * RL).transpose(1, 0, 2)
    sh["rcat_s"] = np.ascontiguousarray(
        np.concatenate([rpart, lpart], axis=2), dtype=np.float32)
    lb = np.asarray(inputs["L0_ffn_lora_B"], dtype=np.float32).reshape(E * RL, D)
    sh["lb_s"] = np.ascontiguousarray(lb.reshape(128, KC, 128))
    bias = np.zeros((128, 6, KC), dtype=np.float32)
    for a, key in enumerate(("L0_attn_bq", "L0_attn_bk", "L0_attn_bv",
                             "L0_attn_bo", "L0_ffn_b_base")):
        bias[:, a, :] = np.asarray(inputs[key], dtype=np.float32).reshape(KC, 128).T
    sh["bias_s"] = bias
    sh["brl_s"] = np.ascontiguousarray(np.tile(
        np.asarray(inputs["L0_ffn_b_router"], dtype=np.float32).reshape(1, E), (128, 1)))
    sh["id_s"] = np.eye(128, dtype=np.float32)
    sh["ones_s"] = np.ones((128, 128), dtype=np.float32)
    return sh


def _core_x(x, c):
    g, r = divmod(c, 4)
    xs = np.asarray(x[g], dtype=np.float32).reshape(KC, 128, D)[r::4]
    tokmaj = xs.reshape(T, D)
    return np.ascontiguousarray(tokmaj.T.reshape(KC, 128, T).transpose(1, 0, 2))


def _core_bmask(c):
    r = c % 4
    m = np.zeros((128, 512), dtype=np.float32)
    qq = np.arange(128)[:, None]
    kk = np.arange(128)[None, :]
    for rp in range(NR):
        if rp == r:
            m[:, rp * 128:(rp + 1) * 128] = np.where(kk <= qq, 0.0, NEG)
        elif rp > r:
            m[:, rp * 128:(rp + 1) * 128] = NEG
    return m


def _in_maps(inputs):
    sh = _prep_shared(inputs)
    x = np.asarray(inputs["L0_attn_x"], dtype=np.float32)
    maps = []
    for c in range(8):
        m = dict(sh)
        m["x_s"] = _core_x(x, c)
        m["bmask_s"] = _core_bmask(c)
        maps.append(m)
    return maps


def _assemble(results):
    out = np.empty((2, 4096, D), dtype=np.float32)
    for c in range(8):
        g, r = divmod(c, 4)
        o = np.asarray(results[c]["out_s"])
        tokmaj = o.transpose(2, 1, 0).reshape(T, D)
        blocks = tokmaj.reshape(NJ, 128, D)
        for j in range(NJ):
            out[g, (4 * j + r) * 128:(4 * j + r + 1) * 128, :] = blocks[j]
    return out


def kernel(**inputs):
    nc = build_program()
    res = run_bass_kernel_spmd(nc, _in_maps(inputs), core_ids=list(range(8)))
    return _assemble(res.results)


if __name__ == "__main__":
    build_program()
    print("build ok")
